# version 16 (final)
# speedup vs baseline: 1.2790x; 1.0005x over previous
import numpy as np
import ml_dtypes
from contextlib import ExitStack

import concourse.bass as bass
import concourse.mybir as mybir
from concourse.bass_utils import run_bass_kernel_spmd

F32 = mybir.dt.float32
BF16 = mybir.dt.bfloat16
AF = mybir.ActivationFunctionType
ALU = mybir.AluOpType
AXX = mybir.AxisListType.X

B, S, D = 4, 4096, 2048
NH, ROPE, NOPE, VDIM = 16, 64, 64, 128
QHEAD = NOPE + ROPE
QLORA, QLORA_P = 682, 768
KVLORA = 256
THETA = 128000.0
SCALE = 1.0 / float(np.sqrt(QHEAD))
EPS_LN = 1e-5
EPS_RMS = 1e-6
NE, DFF = 8, 1024
MYQ = 2048
NEG = -1.0e9
DEBUG = 0
NT, KT, DC, LC = 4, 8, 16, 6
ENGINES = ("sync", "gpsimd", "vector", "scalar", "tensor")
SWAP_MASK = [j for i in range(16) for j in (2 * i + 1, 2 * i)]


class Ctr:
    def __init__(self, sem, step=1):
        self.sem, self.n, self.step = sem, 0, step


def build_program():
    nc = bass.Bass("TRN2", target_bir_lowering=False,
                   detect_race_conditions=False)
    root = ExitStack()

    def dt(name, shape, dtype=F32, kind="ExternalInput"):
        return nc.dram_tensor(name, shape, dtype, kind=kind).ap()

    x_in = dt("x", [S, D])
    xq_in = dt("xq", [MYQ, D])
    qa_w = dt("qa_w", [DC, 128, QLORA_P])
    qa_ln = dt("qa_ln", [128, LC])
    qb_w = dt("qb_w", [NH, 128, LC * 128])
    kva_w = dt("kva_w", [DC, 128, 320])
    kva_ln = dt("kva_ln", [128, 2])
    kvb_w = dt("kvb_w", [2, 128, NH * 192])
    ow_w = dt("ow_w", [DC, 128, D])
    rw_t = dt("rw_t", [DC, 128, NE])
    w1_w = dt("w1", [NE, 8, 128, DC * 128])
    w2_w = dt("w2", [NE, 8, 128, DC * 128])
    w3_w = dt("w3", [NE, 8, 128, D])
    cos_k = dt("cos_k", [128, S])
    sin_k = dt("sin_k", [128, S])
    cos_q = dt("cos_q", [128, MYQ])
    sin_q = dt("sin_q", [128, MYQ])
    graymask = dt("graymask", [8, 128, 512], BF16)
    killf = dt("killf", [NT, 128, 32])
    ident_in = dt("ident_in", [128, 128])

    xn_spill = dt("xn_spill", [DC, 128, MYQ], BF16, "Internal")
    if DEBUG:
        KD = "ExternalOutput"
        dbg_ckv = dt("dbg_ckv", [2, 128, S], BF16, KD)
        dbg_kpe = dt("dbg_kpe", [64, S], BF16, KD)
        dbg_qlat = dt("dbg_qlat", [LC, 128, MYQ], BF16, KD)
        dbg_g = dt("dbg_g", [128, 128], F32, KD)
        dbg_av = dt("dbg_av", [128, NH * MYQ], BF16, KD)
        dbg_kf = dt("dbg_kf", [128, S], BF16, KD)
        dbg_v = dt("dbg_v", [128, S], BF16, KD)
        dbg_qf = dt("dbg_qf", [128, MYQ], BF16, KD)
        dbg_bias = dt("dbg_bias", [128, NT * 32], F32, KD)
        dbg_row1 = dt("dbg_row1", [1, 512], F32, KD)
        dbg_rb = dt("dbg_rb", [128, 512], F32, KD)
        dbg_r2 = dt("dbg_r2", [1, 512], F32, KD)
        dbg_stb = dt("dbg_stb", [128, 512], F32, KD)
        dbg_lnl = dt("dbg_lnl", [1, 512], F32, KD)
        dbg_gray = dt("dbg_gray", [128, 8 * 512], BF16, KD)
        dbg_xnsp = dt("dbg_xnsp", [DC, 128, MYQ], BF16, KD)
        dbg_xnfm0 = dt("dbg_xnfm0", [128, DC * 512], BF16, KD)
        dbg_ckps = dt("dbg_ckps", [128, 512], F32, KD)
        dbg_attn = dt("dbg_attn", [MYQ, D], F32, KD)
    attn_scr = dt("attn_scr", [MYQ, D], F32, "Internal")
    out_hbm = dt("out", [MYQ, D], F32, "ExternalOutput")

    def sbuf(st, name, shape, dtype=F32, side=None):
        if side:
            return st.enter_context(nc.sbuf_tensor(name, shape, dtype,
                                                   side=side))
        return st.enter_context(nc.sbuf_tensor(name, shape, dtype))

    def psum(st, name, shape, dtype=F32):
        return st.enter_context(nc.psum_tensor(name, shape, dtype))

    s_ld = root.enter_context(nc.semaphore("s_ld"))
    s_gp = root.enter_context(nc.semaphore("s_gp"))
    s_t = root.enter_context(nc.semaphore("s_t"))
    s_v = root.enter_context(nc.semaphore("s_v"))
    s_s = root.enter_context(nc.semaphore("s_s"))
    s_p = root.enter_context(nc.semaphore("s_p"))
    s_bar = root.enter_context(nc.semaphore("s_bar"))
    ld, gp = Ctr(s_ld, 16), Ctr(s_gp, 16)
    et, ev, es, ep = Ctr(s_t), Ctr(s_v), Ctr(s_s), Ctr(s_p)

    sched = {e: [] for e in ENGINES}

    def op(engine, fn, inc=None, k=None):
        if inc is None:
            sched[engine].append(lambda eng, fn=fn: fn(eng))
            return None
        kk = inc.step if k is None else k
        inc.n += kk
        sem, val = inc.sem, inc.n

        def wrapped(eng, fn=fn, sem=sem, kk=kk):
            fn(eng).then_inc(sem, kk)

        sched[engine].append(wrapped)
        return val

    def wait(engine, sem, val):
        if val is None:
            return
        if isinstance(sem, Ctr):
            sem = sem.sem
        sched[engine].append(lambda eng, sem=sem, val=val: eng.wait_ge(sem, val))

    def dma(queue, ctr, out, in_):
        return op(queue, lambda eng, out=out, in_=in_:
                  eng.dma_start(out=out, in_=in_), inc=ctr)

    bar_n = [0]

    def barrier():
        bar_n[0] += 1
        tgt = 5 * bar_n[0]

        def _sync(eng, tgt=tgt):
            eng.sem_inc(s_bar, 1)
            eng.wait_ge(s_bar, tgt)
        sched["sync"].append(_sync)

        def _gp(eng, tgt=tgt):
            eng.tensor_copy(bar_sb[0:1, 2:3],
                            bar_sb[0:1, 3:4]).then_inc(s_bar, 1)
            eng.wait_ge(s_bar, tgt)
        sched["gpsimd"].append(_gp)

        def _vec(eng, tgt=tgt):
            eng.tensor_copy(bar_sb[0:1, 0:1],
                            bar_sb[0:1, 3:4]).then_inc(s_bar, 1)
            eng.wait_ge(s_bar, tgt)
        sched["vector"].append(_vec)

        def _scl(eng, tgt=tgt):
            eng.copy(bar_sb[0:1, 1:2], bar_sb[0:1, 3:4]).then_inc(s_bar, 1)
            eng.wait_ge(s_bar, tgt)
        sched["scalar"].append(_scl)

        def _tns(eng, tgt=tgt):
            eng.sem_inc(s_bar, 1)
            eng.wait_ge(s_bar, tgt)
        sched["tensor"].append(_tns)

    def dma_drain():
        wait("sync", s_ld, ld.n)
        wait("gpsimd", s_gp, gp.n)

    ones_col = sbuf(root, "ones_col", [128, 1], BF16)
    ones_row = sbuf(root, "ones_row", [1, 128], F32)
    g_all = sbuf(root, "g_all", [128, 128], F32)
    st1 = sbuf(root, "st1", [128, 40], F32)
    st2 = sbuf(root, "st2", [128, 16], F32)
    row1 = sbuf(root, "row1", [1, 512], F32)
    row2 = sbuf(root, "row2", [1, 512], F32)
    stage_a = sbuf(root, "stage_a", [128, D], F32)
    stage_b = sbuf(root, "stage_b", [128, D], F32)
    bar_sb = sbuf(root, "bar_sb", [1, 8], F32)

    pl = ExitStack()
    ckv_n = [sbuf(pl, f"ckv_n{i}", [128, S], BF16) for i in range(2)]
    kpe = sbuf(pl, "kpe", [128, S], BF16)
    kn2 = sbuf(pl, "kn2", [128, S], BF16)
    qlat = [sbuf(pl, f"qlat{i}", [128, MYQ], BF16) for i in range(LC)]
    qaln_sb = sbuf(pl, "qaln_sb", [128, LC], F32)
    kvln_sb = sbuf(pl, "kvln_sb", [128, 2], F32)

    dma("sync", ld, stage_a[:, 0:128], ident_in)
    dma("sync", ld, stage_a[:, 256:256 + LC], qa_ln)
    v_kl = dma("sync", ld, stage_a[:, 384:386], kva_ln)
    wait("vector", s_ld, v_kl)

    def _init_v(eng):
        eng.tensor_copy(ident[:], stage_a[:, 0:128])
        eng.tensor_copy(qaln_sb[:], stage_a[:, 256:256 + LC])
        eng.tensor_copy(kvln_sb[:], stage_a[:, 384:386])
        eng.memset(ones_col[:], 1.0)
        eng.memset(ones_row[:], 1.0)
        eng.memset(st1[:], 0.0)
        eng.memset(st2[:], 0.0)
        eng.memset(row1[:], 0.0)
        eng.memset(row2[:], 0.0)
        eng.memset(bar_sb[:], 0.0)
    sched["vector"].append(_init_v)
    op("vector", lambda eng: eng.memset(g_all[:], 0.0), inc=ev)
    wait("tensor", s_v, ev.n)
    wait("gpsimd", s_v, ev.n)
    wait("scalar", s_v, ev.n)
    dma_drain()
    barrier()

    p0 = ExitStack()
    xa = [sbuf(p0, f"xa{i}", [128, D], F32) for i in range(2)]
    ident = sbuf(p0, "ident", [128, 128], BF16)
    xn_tok = sbuf(p0, "xn_tok", [128, D], BF16)
    sq_junk = sbuf(p0, "sq_junk", [128, D], BF16)
    xn_fm = sbuf(p0, "xn_fm", [128, DC * 512], BF16)
    kva_bf = sbuf(p0, "kva_bf", [128, DC * 320], BF16)
    qa_bf = sbuf(p0, "qa_bf", [128, DC * QLORA_P], BF16)
    rw_bf = sbuf(p0, "rw_bf", [128, DC * NE], BF16)
    qlraw = sbuf(p0, "qlraw", [128, LC * 512], F32)
    rb_sb = sbuf(p0, "rb_sb", [128, 512], F32)
    sq_sb = sbuf(p0, "sq_sb", [128, 512], BF16)
    lg_sb = sbuf(p0, "lg_sb", [128, NE], F32)
    lg_t1 = sbuf(p0, "lg_t1", [128, NE], F32)
    lg_t2 = sbuf(p0, "lg_t2", [128, NE], F32)
    pe_sb = sbuf(p0, "pe_sb", [128, 512], F32)
    pe_sw = sbuf(p0, "pe_sw", [128, 512], F32)
    pe_t1 = sbuf(p0, "pe_t1", [128, 512], F32)
    cosk_sb = [sbuf(p0, f"cosk{i}", [128, 512], F32) for i in range(2)]
    sink_sb = [sbuf(p0, f"sink{i}", [128, 512], F32) for i in range(2)]

    p0p = ExitStack()
    tp_ps = [psum(p0p, f"tp_ps{i}", [128, 512], BF16) for i in range(2)]
    ck_ps = [psum(p0p, f"ck_ps{i}", [128, 512], F32) for i in range(2)]
    pe_ps = psum(p0p, "pe_ps", [128, 512], F32)
    ss_ps = psum(p0p, "ss_ps", [1, 512], F32)
    rb_ps = psum(p0p, "rb_ps", [128, 512], F32)
    lg_ps = rb_ps[:, 0:NE]

    p0_stage_guard = {0: None, 1: None}

    def cast_chunks(dst, dst_cols, src3d, n, cols):
        B = max(1, D // cols)
        r, i = 0, 0
        while i < n:
            m = min(B, n - i)
            st = (stage_a, stage_b)[r % 2]
            wait("gpsimd", s_s, p0_stage_guard[r % 2])
            v = None
            for j in range(m):
                v = dma("gpsimd", gp, st[:, j * cols:(j + 1) * cols],
                        src3d[i + j])
            wait("scalar", s_gp, v)
            p0_stage_guard[r % 2] = op(
                "scalar",
                lambda eng, i=i, m=m, st=st, cols=cols:
                eng.copy(dst[:, i * dst_cols:i * dst_cols + m * cols],
                         st[:, 0:m * cols]),
                inc=es)
            r += 1
            i += m
        return es.n

    v_kvacast = cast_chunks(kva_bf, 320, kva_w, DC, 320)
    wait("tensor", s_s, v_kvacast)

    xa_free = [None, None]
    xn_tok_free = [None]
    tp_free = [None, None]
    xnfm_free = [None]
    xnfm_spill = [None]
    ck_free = [None, s_v]
    pe_free = [None]
    pe_sb_free = [None]
    ss_free = [None]
    rb_free = [None]
    rbsb_free = [None]
    lg_free = [None]
    cos_free = [None, None]
    sq_free = [None]
    ql_evac = [None]

    def ln_pair(row0, src_dram):
        wait("sync", s_s, xa_free[0])
        vx0 = dma("sync", ld, xa[0][:], src_dram[row0:row0 + 128, :])
        wait("sync", s_s, xa_free[1])
        vx1 = dma("sync", ld, xa[1][:], src_dram[row0 + 128:row0 + 256, :])
        wait("vector", s_ld, vx0)
        op("vector", lambda eng: eng.reduce_sum(
            st1[:, 0:1], xa[0][:], axis=AXX))
        wait("vector", s_ld, vx1)
        v_rs = op("vector", lambda eng: eng.reduce_sum(
            st1[:, 1:2], xa[1][:], axis=AXX), inc=ev)
        wait("vector", s_v, v_rs)
        wait("scalar", s_ld, vx0)
        op("scalar", lambda eng: eng.activation(
            sq_junk[:], xa[0][:], AF.Square, accum_out=st1[:, 2:3]))
        wait("scalar", s_ld, vx1)
        v_sq = op("scalar", lambda eng: eng.activation(
            sq_junk[:], xa[1][:], AF.Square, accum_out=st1[:, 3:4]), inc=es)
        wait("vector", s_s, v_sq)

        def vstep(fn):
            v = op("vector", fn, inc=ev)
            wait("vector", s_v, v)
            return v

        vstep(lambda eng: eng.tensor_scalar(
            st1[:, 4:6], st1[:, 0:2], -1.0 / D, None, ALU.mult))
        vstep(lambda eng: eng.tensor_scalar(
            st1[:, 6:8], st1[:, 0:2], 1.0 / D, None, ALU.mult))
        vstep(lambda eng: eng.tensor_scalar(
            st1[:, 8:10], st1[:, 2:4], 1.0 / D, None, ALU.mult))
        vstep(lambda eng: eng.tensor_tensor(
            st1[:, 10:12], st1[:, 6:8], st1[:, 6:8], ALU.mult))
        vstep(lambda eng: eng.tensor_tensor(
            st1[:, 12:14], st1[:, 8:10], st1[:, 10:12], ALU.subtract))
        vstep(lambda eng: eng.tensor_scalar(
            st1[:, 14:16], st1[:, 12:14], 1.0, EPS_LN, ALU.mult, ALU.add))
        v_rc = op("vector", lambda eng: eng.reciprocal(
            st1[:, 16:18], st1[:, 14:16]), inc=ev)
        wait("scalar", s_v, v_rc)
        v1 = op("scalar", lambda eng: eng.activation(
            st2[:, 0:2], st1[:, 16:18], AF.Sqrt), inc=es)
        wait("vector", s_s, v1)
        v2 = op("vector", lambda eng: eng.tensor_tensor(
            st2[:, 2:4], st1[:, 4:6], st2[:, 0:2], ALU.mult), inc=ev)
        wait("scalar", s_v, v2)
        return v2

    def ln_apply(bi):
        wait("scalar", s_t, xn_tok_free[0])
        v_xn = op("scalar", lambda eng, b=bi: eng.activation(
            xn_tok[:], xa[b][:], AF.Identity,
            bias=st2[:, 2 + b:3 + b], scale=st2[:, 0 + b:1 + b]), inc=es)
        xa_free[bi] = v_xn
        return v_xn

    def transpose_subtile(u, v_xn):
        wait("tensor", s_s, v_xn)
        if u == 0:
            wait("scalar", s_t, xnfm_free[0])
            wait("scalar", s_ld, xnfm_spill[0])
        t_last = None
        for grp in range(4):
            wait("tensor", s_v, tp_free[grp % 2])
            for j in range(4):
                dc = grp * 4 + j
                t_last = op("tensor", lambda eng, g=grp, j=j, dc=dc:
                            eng.transpose(
                                tp_ps[g % 2][:, j * 128:(j + 1) * 128],
                                xn_tok[:, dc * 128:(dc + 1) * 128], ident[:]),
                            inc=(et if j == 3 else None))
            wait("vector", s_t, t_last)
            dst = xn_fm[:].rearrange("p (dc j) -> p dc j", j=512)[
                :, grp * 4:(grp + 1) * 4, u * 128:(u + 1) * 128]
            src = tp_ps[grp % 2][:].rearrange("p (dc j) -> p dc j", j=128)
            tp_free[grp % 2] = op(
                "vector", lambda eng, dst=dst, src=src: eng.tensor_copy(
                    dst, src), inc=ev)
        xn_tok_free[0] = t_last
        return ev.n

    def rms_rstd(n_feat):
        t_ss = et.n
        wait("scalar", s_t, t_ss)
        v_row = op("scalar", lambda eng: eng.copy(row1[:], ss_ps[:]), inc=es)
        ss_free[0] = v_row
        wait("vector", s_s, v_row)
        vr = op("vector", lambda eng, n=n_feat: eng.tensor_scalar(
            row1[:], row1[:], 1.0 / n, EPS_RMS, ALU.mult, ALU.add), inc=ev)
        wait("vector", s_v, vr)
        v_rc = op("vector", lambda eng: eng.reciprocal(row2[:], row1[:]),
                  inc=ev)
        wait("scalar", s_v, v_rc)
        v_rstd = op("scalar", lambda eng: eng.activation(
            row1[:], row2[:], AF.Sqrt), inc=es)
        wait("tensor", s_s, v_rstd)
        wait("tensor", s_s, rb_free[0])
        wait("tensor", s_v, lg_free[0])
        t_rb = op("tensor", lambda eng: eng.matmul(
            rb_ps[:], ones_row[:], row1[:], start=True, stop=True), inc=et)
        wait("scalar", s_t, t_rb)
        wait("scalar", s_v, rbsb_free[0])
        v_rb = op("scalar", lambda eng: eng.copy(rb_sb[:], rb_ps[:]), inc=es)
        rb_free[0] = v_rb
        return v_rb

    def kv_tile(T, v_fm):
        wait("tensor", s_v, v_fm)
        if DEBUG and T == 0:
            wait("sync", s_v, v_fm)
            dma("sync", ld, dbg_xnfm0, xn_fm[:])
        bb = T % 2
        wait("sync", s_v, cos_free[bb])
        vcs = dma("sync", ld, cosk_sb[bb][:], cos_k[:, T * 512:(T + 1) * 512])
        vsn = dma("sync", ld, sink_sb[bb][:], sin_k[:, T * 512:(T + 1) * 512])
        for m in range(3):
            if m == 0:
                wait("tensor", ck_free[1], ck_free[0])
            if m == 2:
                wait("tensor", s_s, pe_free[0])
            t_mm = None
            for dc in range(DC):
                def _mm(eng, m=m, dc=dc):
                    if m < 2:
                        o, c0, cw, tp = ck_ps[m][:], m * 128, 128, None
                    else:
                        o, c0, cw, tp = pe_ps[64:128, :], 256, 64, (0, 64)
                    return eng.matmul(
                        o, kva_bf[:, dc * 320 + c0:dc * 320 + c0 + cw],
                        xn_fm[:, dc * 512:(dc + 1) * 512],
                        start=(dc == 0), stop=(dc == DC - 1),
                        tile_position=tp)
                t_mm = op("tensor", _mm, inc=(et if dc == DC - 1 else None))
            if m < 2:
                wait("scalar", s_t, t_mm)
                if DEBUG and T == 0 and m == 0:
                    v_ck_evt = op("scalar", lambda eng: eng.copy(
                        stage_b[:, 1024:1536], ck_ps[0][:]), inc=es)
                    wait("sync", s_s, v_ck_evt)
                    dma("sync", ld, dbg_ckps, stage_b[:, 1024:1536])
                wait("scalar", s_t, sq_free[0])
                v_sqk = op("scalar", lambda eng, m=m: eng.activation(
                    sq_sb[:], ck_ps[m][:], AF.Square), inc=es)
                wait("tensor", s_s, v_sqk)
                if m == 0:
                    wait("tensor", s_s, ss_free[0])
                sq_free[0] = op("tensor", lambda eng, m=m: eng.matmul(
                    ss_ps[:], ones_col[:], sq_sb[:],
                    start=(m == 0), stop=(m == 1)), inc=et)
            else:
                wait("scalar", s_t, t_mm)
                wait("scalar", s_v, pe_sb_free[0])
                v_pe = op("scalar", lambda eng: eng.copy(
                    pe_sb[64:128, :], pe_ps[64:128, :]), inc=es)
                pe_free[0] = v_pe
                wait("vector", s_s, v_pe)
                wait("vector", s_ld, max(vcs, vsn))
                vsh = op("vector", lambda eng: eng.stream_shuffle(
                    pe_sw[64:128, :], pe_sb[64:128, :], SWAP_MASK), inc=ev)
                pe_sb_free[0] = op("vector", lambda eng, bb=bb:
                                   eng.tensor_tensor(
                                       pe_t1[64:128, :], pe_sb[64:128, :],
                                       cosk_sb[bb][64:128, :], ALU.mult),
                                   inc=ev)
                wait("vector", s_v, vsh)
                cos_free[bb] = op("vector", lambda eng, bb=bb:
                                  eng.tensor_tensor(
                                      pe_sw[64:128, :], pe_sw[64:128, :],
                                      sink_sb[bb][64:128, :], ALU.mult),
                                  inc=ev)
                wait("vector", s_v, cos_free[bb])
                vk = op("vector", lambda eng, T=T: eng.tensor_tensor(
                    kpe[64:128, T * 512:(T + 1) * 512],
                    pe_t1[64:128, :], pe_sw[64:128, :], ALU.add), inc=ev)
                wait("vector", s_v, vk)
                op("vector", lambda eng, T=T: eng.tensor_tensor(
                    kn2[64:128, T * 512:(T + 1) * 512],
                    kpe[64:128, T * 512:(T + 1) * 512],
                    kpe[64:128, T * 512:(T + 1) * 512], ALU.mult), inc=ev)
        v_rb = rms_rstd(KVLORA)
        wait("vector", s_s, v_rb)
        vv = None
        for cc in range(2):
            vv = op("vector", lambda eng, cc=cc, T=T:
                    eng.scalar_tensor_tensor(
                        ckv_n[cc][:, T * 512:(T + 1) * 512],
                        ck_ps[cc][:], kvln_sb[:, cc:cc + 1], rb_sb[:],
                        ALU.mult, ALU.mult), inc=ev)
        ck_free[0], ck_free[1] = vv, s_v
        rbsb_free[0] = vv
        xnfm_free[0] = et.n

    def q_tile(T, v_fm):
        wait("tensor", s_v, v_fm)
        wait("sync", s_v, v_fm)
        sp = None
        for dc in range(DC):
            sp = dma("sync", ld, xn_spill[dc, :, T * 512:(T + 1) * 512],
                     xn_fm[:, dc * 512:(dc + 1) * 512])
        xnfm_spill[0] = sp
        sq_done = [None, None]
        for lc in range(LC):
            if lc < 2:
                wait("tensor", ck_free[1], ck_free[0])
            else:
                wait("tensor", s_s, sq_done[lc % 2])
            t_mm = None
            for dc in range(DC):
                t_mm = op("tensor", lambda eng, lc=lc, dc=dc: eng.matmul(
                    ck_ps[lc % 2][:],
                    qa_bf[:, dc * QLORA_P + lc * 128:
                          dc * QLORA_P + (lc + 1) * 128],
                    xn_fm[:, dc * 512:(dc + 1) * 512],
                    start=(dc == 0), stop=(dc == DC - 1)),
                    inc=(et if dc == DC - 1 else None))
            wait("scalar", s_t, t_mm)
            if lc == 0:
                wait("scalar", s_v, ql_evac[0])
            op("scalar", lambda eng, lc=lc: eng.copy(
                qlraw[:, lc * 512:(lc + 1) * 512], ck_ps[lc % 2][:]))
            wait("scalar", s_t, sq_free[0])
            v_sq = op("scalar", lambda eng, lc=lc: eng.activation(
                sq_sb[:], ck_ps[lc % 2][:], AF.Square), inc=es)
            sq_done[lc % 2] = v_sq
            wait("tensor", s_s, v_sq)
            if lc == 0:
                wait("tensor", s_s, ss_free[0])
            sq_free[0] = op("tensor", lambda eng, lc=lc: eng.matmul(
                ss_ps[:], ones_col[:], sq_sb[:],
                start=(lc == 0), stop=(lc == LC - 1)), inc=et)
        v_rb = rms_rstd(QLORA)
        wait("vector", s_s, v_rb)
        vv = None
        for lc in range(LC):
            vv = op("vector", lambda eng, lc=lc, T=T:
                    eng.scalar_tensor_tensor(
                        qlat[lc][:, T * 512:(T + 1) * 512],
                        qlraw[:, lc * 512:(lc + 1) * 512],
                        qaln_sb[:, lc:lc + 1], rb_sb[:],
                        ALU.mult, ALU.mult), inc=ev)
        ql_evac[0] = vv
        ck_free[0], ck_free[1] = sq_done[(LC - 1) % 2], s_s
        rbsb_free[0] = vv
        wait("tensor", s_s, v_rb)
        for u in range(4):
            sub = T * 4 + u
            wait("tensor", s_v, lg_free[0])
            t_lg = None
            for dc in range(DC):
                t_lg = op("tensor", lambda eng, u=u, dc=dc: eng.matmul(
                    lg_ps,
                    xn_fm[:, dc * 512 + u * 128:dc * 512 + (u + 1) * 128],
                    rw_bf[:, dc * NE:(dc + 1) * NE],
                    start=(dc == 0), stop=(dc == DC - 1)),
                    inc=(et if dc == DC - 1 else None))
            wait("vector", s_t, t_lg)

            def vstep2(fn):
                v = op("vector", fn, inc=ev)
                wait("vector", s_v, v)
                return v

            lg_free[0] = vstep2(lambda eng: eng.tensor_copy(
                lg_sb[:], lg_ps))
            vstep2(lambda eng: eng.reduce_max(
                st1[:, 20:21], lg_sb[:], axis=AXX))
            vstep2(lambda eng: eng.tensor_scalar(
                st1[:, 21:22], st1[:, 20:21], -1.0, None, ALU.mult))
            vstep2(lambda eng: eng.tensor_scalar(
                lg_t1[:], lg_sb[:], st1[:, 20:21], None, ALU.is_ge))
            vstep2(lambda eng: eng.scalar_tensor_tensor(
                lg_t2[:], lg_t1[:], -1.0e30, lg_sb[:], ALU.mult, ALU.add))
            v_m2 = vstep2(lambda eng: eng.reduce_max(
                st1[:, 22:23], lg_t2[:], axis=AXX))
            wait("scalar", s_v, v_m2)
            v_ex = op("scalar", lambda eng: eng.activation(
                lg_t2[:], lg_sb[:], AF.Exp, bias=st1[:, 21:22]), inc=es)
            wait("vector", s_s, v_ex)
            vstep2(lambda eng: eng.tensor_scalar(
                lg_t1[:], lg_sb[:], st1[:, 22:23], None, ALU.is_ge))
            vstep2(lambda eng: eng.tensor_tensor(
                lg_t1[:], lg_t1[:], lg_t2[:], ALU.mult))
            vstep2(lambda eng: eng.reduce_sum(
                st1[:, 23:24], lg_t1[:], axis=AXX))
            vstep2(lambda eng: eng.reciprocal(
                st1[:, 24:25], st1[:, 23:24]))
            op("vector", lambda eng, sub=sub: eng.tensor_scalar(
                g_all[:, sub * 8:(sub + 1) * 8], lg_t1[:],
                st1[:, 24:25], None, ALU.mult), inc=ev)
        xnfm_free[0] = et.n

    for T in range(KT):
        v_fm = None
        for up in range(2):
            ln_pair(T * 512 + up * 256, x_in)
            for hu in range(2):
                v_xn = ln_apply(hu)
                v_fm = transpose_subtile(up * 2 + hu, v_xn)
        if DEBUG == 2:
            dma_drain()
            barrier()
        kv_tile(T, v_fm)
        if DEBUG == 2:
            dma_drain()
            barrier()
        if T == 0:
            cast_chunks(qa_bf, QLORA_P, qa_w, DC, QLORA_P)
            v_wcast = cast_chunks(rw_bf, NE, rw_t, DC, NE)
            wait("tensor", s_s, v_wcast)
    for T in range(NT):
        v_fm = None
        for up in range(2):
            ln_pair(T * 512 + up * 256, xq_in)
            for hu in range(2):
                v_xn = ln_apply(hu)
                v_fm = transpose_subtile(up * 2 + hu, v_xn)
        q_tile(T, v_fm)

    dma_drain()
    barrier()
    if DEBUG:
        dma("sync", ld, dbg_row1, row1[:])
        dma("sync", ld, dbg_rb, rb_sb[:])
        for cc in range(2):
            dma("sync", ld, dbg_ckv[cc], ckv_n[cc][:])
        dma("sync", ld, dbg_kpe, kpe[64:128, :])
        for lc in range(LC):
            dma("sync", ld, dbg_qlat[lc], qlat[lc][:])
        dma("sync", ld, dbg_g, g_all[:])
        dma_drain()
        barrier()
    p0p.close()
    p0.close()

    pav = ExitStack()
    out_av = sbuf(pav, "out_av", [128, NH * MYQ], BF16, side="right")
    pa = ExitStack()
    kf = sbuf(pa, "kf", [128, S], BF16)
    v_sb = sbuf(pa, "v_sb", [128, S], BF16)
    qf = sbuf(pa, "qf", [128, MYQ], BF16)
    q2_sb = sbuf(pa, "q2_sb", [128, 512], BF16)
    expT = [sbuf(pa, f"expT{i}", [128, 512], BF16) for i in range(4)]
    qb_bf = sbuf(pa, "qb_bf", [128, LC * 128], BF16)
    kvb_bf = [sbuf(pa, f"kvb_bf{i}", [128, NH * 192], BF16) for i in range(2)]
    cosq_sb = sbuf(pa, "cosq_sb", [128, MYQ], F32)
    sinq_sb = sbuf(pa, "sinq_sb", [128, MYQ], F32)
    gray_bf = sbuf(pa, "gray_bf", [128, 8 * 512], BF16)
    ones_mat = sbuf(pa, "ones_mat", [128, 128], BF16)
    kill_sb = sbuf(pa, "kill_sb", [128, NT * 32], F32)
    bias_sb = sbuf(pa, "bias_sb", [128, NT * 32], F32)
    pa_pe = stage_a[:, 1024:1536]
    pa_sw = stage_a[:, 1536:2048]
    pa_t1 = stage_b[:, 1024:1536]

    pap = ExitStack()
    sc_ps = [psum(pap, f"sc_ps{i}", [128, 512], F32) for i in range(4)]
    av_ps = [psum(pap, f"av_ps{i}", [128, 512], F32) for i in range(2)]
    l_ps = psum(pap, "l_ps", [128, 512], F32)
    rbn_ps = psum(pap, "rbn_ps", [128, 512], F32)

    dma("sync", ld, cosq_sb[:], cos_q)
    v_cosq = dma("sync", ld, sinq_sb[:], sin_q)
    stg_cast = {0: None, 1: None}
    for cc in range(2):
        wait("gpsimd", s_s, stg_cast[0])
        wait("gpsimd", s_s, stg_cast[1])
        dma("gpsimd", gp, stage_a[:, 0:2048], kvb_w[cc][:, 0:2048])
        vw = dma("gpsimd", gp, stage_b[:, 0:1024], kvb_w[cc][:, 2048:3072])
        wait("scalar", s_gp, vw)
        op("scalar", lambda eng, cc=cc: eng.copy(
            kvb_bf[cc][:, 0:2048], stage_a[:, 0:2048]))
        stg_cast[0] = stg_cast[1] = op("scalar", lambda eng, cc=cc: eng.copy(
            kvb_bf[cc][:, 2048:3072], stage_b[:, 0:1024]), inc=es)
    v_ks = None
    for gm in range(8):
        v_ks = dma("sync", ld, gray_bf[:, gm * 512:(gm + 1) * 512],
                   graymask[gm])
    for tt in range(NT):
        v_ks = dma("sync", ld, kill_sb[:, tt * 32:(tt + 1) * 32], killf[tt])
    wait("scalar", s_ld, v_ks)
    v_pa_ld = max(stg_cast[0], stg_cast[1])
    wait("vector", s_s, v_pa_ld)
    wait("vector", s_ld, v_ks)
    wait("scalar", s_ld, v_ks)
    wait("vector", s_ld, v_cosq)
    wait("tensor", s_s, v_pa_ld)
    op("vector", lambda eng: eng.memset(ones_mat[:], 1.0), inc=ev)
    wait("tensor", s_v, ev.n)
    kf_pe_evt = op("gpsimd", lambda eng: eng.tensor_copy(
        kf[64:128, :], kpe[64:128, :]), inc=ep)
    wait("tensor", s_p, kf_pe_evt)

    sc_guard = [(None, s_s)] * 4
    l_guard = (None, s_s)
    av_guard = [(None, s_v), (None, s_v)]
    rbn_guard = (None, s_s)
    qb_cast_evt = [None]
    last_exp = [None]
    bias_done = [None]

    for h in range(NH):
        wait("gpsimd", s_s, qb_cast_evt[0] if h > 0 else v_pa_ld)
        v_qb = dma("gpsimd", gp, stage_a[:, 0:LC * 128], qb_w[h])
        wait("scalar", s_gp, v_qb)
        qb_cast_evt[0] = op("scalar", lambda eng: eng.copy(
            qb_bf[:], stage_a[:, 0:LC * 128]), inc=es)

        wait("tensor", s_s, qb_cast_evt[0])
        rope_evt = [None]
        nope_evt = [None]
        for qt in range(NT):
            bi = qt % 2
            wait("tensor", sc_guard[bi][1], sc_guard[bi][0])
            t_mm = None
            for lc in range(LC):
                t_mm = op("tensor", lambda eng, qt=qt, lc=lc, bi=bi:
                          eng.matmul(
                              sc_ps[bi][:],
                              qb_bf[:, lc * 128:(lc + 1) * 128],
                              qlat[lc][:, qt * 512:(qt + 1) * 512],
                              start=(lc == 0), stop=(lc == LC - 1)),
                          inc=(et if lc == LC - 1 else None))
            wait("scalar", s_t, t_mm)
            nope_evt[0] = op("scalar", lambda eng, qt=qt, bi=bi: eng.copy(
                qf[0:64, qt * 512:(qt + 1) * 512], sc_ps[bi][0:64, :]),
                inc=es)
            wait("scalar", s_v, rope_evt[0])
            v_pe = op("scalar", lambda eng, bi=bi: eng.copy(
                pa_pe[64:128, :], sc_ps[bi][64:128, :]), inc=es)
            sc_guard[bi] = (v_pe, s_s)
            wait("vector", s_s, v_pe)
            vsh = op("vector", lambda eng: eng.stream_shuffle(
                pa_sw[64:128, :], pa_pe[64:128, :], SWAP_MASK), inc=ev)
            vt1 = op("vector", lambda eng, qt=qt: eng.tensor_tensor(
                pa_t1[64:128, :], pa_pe[64:128, :],
                cosq_sb[64:128, qt * 512:(qt + 1) * 512], ALU.mult), inc=ev)
            wait("vector", s_v, vsh)
            vt2 = op("vector", lambda eng, qt=qt: eng.tensor_tensor(
                pa_sw[64:128, :], pa_sw[64:128, :],
                sinq_sb[64:128, qt * 512:(qt + 1) * 512], ALU.mult), inc=ev)
            wait("vector", s_v, vt2)
            rope_evt[0] = op("vector", lambda eng, qt=qt: eng.tensor_tensor(
                qf[64:128, qt * 512:(qt + 1) * 512],
                pa_t1[64:128, :], pa_sw[64:128, :], ALU.add), inc=ev)

        knope_evac = None
        for kt in range(KT):
            bi = kt % 2
            wait("tensor", sc_guard[bi][1], sc_guard[bi][0])
            t_mm = None
            for cc in range(2):
                t_mm = op("tensor", lambda eng, kt=kt, cc=cc, bi=bi, h=h:
                          eng.matmul(
                              sc_ps[bi][0:64, :],
                              kvb_bf[cc][:, h * 192:h * 192 + 64],
                              ckv_n[cc][:, kt * 512:(kt + 1) * 512],
                              start=(cc == 0), stop=(cc == 1)),
                          inc=(et if cc == 1 else None))
            wait("scalar", s_t, t_mm)
            knope_evac = op("scalar", lambda eng, kt=kt, bi=bi: eng.copy(
                kf[0:64, kt * 512:(kt + 1) * 512], sc_ps[bi][0:64, :]),
                inc=es)
            sc_guard[bi] = (knope_evac, s_s)

        for kg in range(8):
            bi = kg % 2
            wait("tensor", sc_guard[bi][1], sc_guard[bi][0])
            t_mm = None
            for sub4 in range(4):
                kt = kg * 4 + sub4
                for cc in range(2):
                    t_mm = op("tensor",
                              lambda eng, kt=kt, cc=cc, bi=bi, h=h, s4=sub4:
                              eng.matmul(
                                  sc_ps[bi][:, s4 * 128:(s4 + 1) * 128],
                                  ckv_n[cc][:, kt * 128:(kt + 1) * 128],
                                  kvb_bf[cc][:, h * 192 + 64:h * 192 + 192],
                                  start=(cc == 0), stop=(cc == 1)),
                              inc=(et if (cc == 1 and sub4 == 3) else None))
            wait("scalar", s_t, t_mm)
            v_ev = op("scalar", lambda eng, kg=kg, bi=bi: eng.copy(
                v_sb[:, kg * 512:(kg + 1) * 512], sc_ps[bi][:]), inc=es)
            sc_guard[bi] = (v_ev, s_s)

        wait("vector", s_s, es.n)
        v_kn2 = op("vector", lambda eng: eng.tensor_tensor(
            kn2[0:64, :], kf[0:64, :], kf[0:64, :], ALU.mult), inc=ev)
        v_ms = op("vector", lambda eng: eng.memset(row1[0:1, 0:4], -3.0e38),
                  inc=ev)
        wait("vector", s_v, v_ms)
        wait("tensor", s_v, v_kn2)
        wait("tensor", l_guard[1], l_guard[0])
        mx_evt = [None, None]
        for kc in range(8):
            pb = (l_ps, rbn_ps)[kc % 2]
            wait("tensor", s_v, mx_evt[kc % 2])
            t_k2 = op("tensor", lambda eng, kc=kc, pb=pb: eng.matmul(
                pb[:], ones_mat[:], kn2[:, kc * 512:(kc + 1) * 512],
                start=True, stop=True), inc=et)
            wait("vector", s_t, t_k2)
            mx_evt[kc % 2] = op(
                "vector", lambda eng, pb=pb, kc=kc: eng.reduce_max(
                    row1[0:1, 8 + kc:9 + kc], pb[0:1, :], axis=AXX), inc=ev)
        wait("vector", s_s, nope_evt[0])
        tq2_evt = [None]
        for qc in range(NT):
            pb = (l_ps, rbn_ps)[qc % 2]
            wait("vector", s_t, tq2_evt[0])
            v_q2 = op("vector", lambda eng, qc=qc: eng.tensor_tensor(
                q2_sb[:], qf[:, qc * 512:(qc + 1) * 512],
                qf[:, qc * 512:(qc + 1) * 512], ALU.mult), inc=ev)
            wait("tensor", s_v, v_q2)
            wait("tensor", s_v, mx_evt[qc % 2])
            t_q2 = op("tensor", lambda eng, pb=pb: eng.matmul(
                pb[:], ones_mat[:], q2_sb[:],
                start=True, stop=True), inc=et)
            tq2_evt[0] = t_q2
            wait("vector", s_t, t_q2)
            mx_evt[qc % 2] = op(
                "vector", lambda eng, pb=pb, qc=qc: eng.reduce_max(
                    row1[0:1, 16 + qc:17 + qc], pb[0:1, :], axis=AXX), inc=ev)
        l_guard = (mx_evt[(NT - 1) % 2], s_v)
        rbn_guard = (mx_evt[NT % 2], s_v)
        wait("vector", s_v, mx_evt[0])
        wait("vector", s_v, mx_evt[1])
        v_k8 = op("vector", lambda eng: eng.reduce_max(
            row1[0:1, 0:1], row1[0:1, 8:16], axis=AXX), inc=ev)
        wait("vector", s_v, v_k8)
        v_q4 = op("vector", lambda eng: eng.reduce_max(
            row1[0:1, 1:2], row1[0:1, 16:16 + NT], axis=AXX), inc=ev)
        wait("vector", s_v, v_q4)
        vmm = op("vector", lambda eng: eng.tensor_tensor(
            row1[0:1, 5:6], row1[0:1, 0:1], row1[0:1, 1:2], ALU.mult),
            inc=ev)
        wait("vector", s_v, vmm)
        v_mm2 = op("vector", lambda eng: eng.tensor_scalar(
            row1[0:1, 5:6], row1[0:1, 5:6], 1.0, 1e-12, ALU.mult, ALU.add),
            inc=ev)
        wait("scalar", s_v, v_mm2)
        v_sqB = op("scalar", lambda eng: eng.activation(
            row1[0:1, 6:7], row1[0:1, 5:6], AF.Sqrt), inc=es)
        wait("vector", s_s, v_sqB)
        v_nB = op("vector", lambda eng: eng.tensor_scalar(
            row1[0:1, 7:8], row1[0:1, 6:7], -SCALE * 1.03, -1.0,
            ALU.mult, ALU.add), inc=ev)
        wait("tensor", s_v, v_nB)
        wait("tensor", rbn_guard[1], rbn_guard[0])
        t_nb = op("tensor", lambda eng: eng.matmul(
            rbn_ps[:, 0:1], ones_row[:], row1[0:1, 7:8],
            start=True, stop=True), inc=et)
        wait("scalar", s_t, t_nb)
        wait("scalar", s_v, bias_done[0])
        v_nbc = op("scalar", lambda eng: eng.copy(
            st2[:, 2:3], rbn_ps[:, 0:1]), inc=es)
        rbn_guard = (v_nbc, s_s)
        wait("vector", s_s, v_nbc)
        wait("vector", s_s, last_exp[0])
        v_bias = None
        for k in range(NT):
            v_bias = op("vector", lambda eng, k=k: eng.tensor_scalar(
                bias_sb[:, k * 32:(k + 1) * 32],
                kill_sb[:, k * 32:(k + 1) * 32],
                NEG, st2[:, 2:3], ALU.mult, ALU.add), inc=ev)
        bias_done[0] = v_bias
        wait("scalar", s_v, v_bias)

        for k in range(NT):
            Ck = 8 * k + 8
            ab = k % 2
            exp_evts = {}

            def emit_score(c, k=k):
                i3 = c % 4
                wait("tensor", sc_guard[i3][1], sc_guard[i3][0])
                t_sc = op("tensor", lambda eng, c=c, k=k, i3=i3: eng.matmul(
                    sc_ps[i3][:], kf[:, c * 128:(c + 1) * 128],
                    qf[:, k * 512:(k + 1) * 512],
                    start=True, stop=True), inc=et)
                if c >= 8 * k:
                    g = c - 8 * k
                    W = 128 * ((g % 4) + 1)
                    wait("vector", s_t, t_sc)
                    v_mk = op("vector", lambda eng, g=g, W=W, i3=i3:
                              eng.tensor_tensor(
                                  sc_ps[i3][:, 0:W], sc_ps[i3][:, 0:W],
                                  gray_bf[:, g * 512:g * 512 + W], ALU.add),
                              inc=ev)
                    wait("scalar", s_v, v_mk)
                else:
                    wait("scalar", s_t, t_sc)
                v_exp = op(
                    "scalar", lambda eng, i3=i3, k=k, c=c: eng.activation(
                        expT[i3][:], sc_ps[i3][:], AF.Exp,
                        bias=bias_sb[:, k * 32 + c:k * 32 + c + 1],
                        scale=SCALE), inc=es)
                sc_guard[i3] = (v_exp, s_s)
                last_exp[0] = v_exp
                exp_evts[c] = v_exp

            emit_score(0)
            if Ck > 1:
                emit_score(1)
            if Ck > 2:
                emit_score(2)
            for c in range(Ck):
                if c + 3 < Ck:
                    emit_score(c + 3)
                wait("tensor", s_s, exp_evts.pop(c))
                if c == 0:
                    wait("tensor", l_guard[1], l_guard[0])
                    wait("tensor", av_guard[ab][1], av_guard[ab][0])
                i3 = c % 4
                op("tensor", lambda eng, i3=i3, c=c, Ck=Ck: eng.matmul(
                    l_ps[:], ones_mat[:], expT[i3][:],
                    start=(c == 0), stop=(c == Ck - 1)))
                op("tensor", lambda eng, i3=i3, c=c, Ck=Ck, ab=ab: eng.matmul(
                    av_ps[ab][:], v_sb[:, c * 128:(c + 1) * 128], expT[i3][:],
                    start=(c == 0), stop=(c == Ck - 1)),
                    inc=(et if c == Ck - 1 else None))
            t_av = et.n
            wait("scalar", s_t, t_av)
            v_lc = op("scalar", lambda eng: eng.copy(row2[:], l_ps[0:1, :]),
                      inc=es)
            l_guard = (v_lc, s_s)
            wait("scalar", s_s, v_lc)
            v_ln = op("scalar", lambda eng: eng.activation(
                row1[:], row2[:], AF.Ln), inc=es)
            wait("scalar", s_s, v_ln)
            v_r = op("scalar", lambda eng: eng.activation(
                row2[:], row1[:], AF.Exp, scale=-1.0), inc=es)
            wait("tensor", s_s, v_r)
            wait("tensor", rbn_guard[1], rbn_guard[0])
            t_rb = op("tensor", lambda eng: eng.matmul(
                rbn_ps[:], ones_row[:], row2[:], start=True, stop=True),
                inc=et)
            wait("scalar", s_t, t_rb)
            v_rb2 = op("scalar", lambda eng, ab=ab: eng.copy(
                stage_b[:, ab * 512:(ab + 1) * 512], rbn_ps[:]), inc=es)
            rbn_guard = (v_rb2, s_s)
            wait("vector", s_s, v_rb2)
            v_an = op("vector", lambda eng, h=h, k=k, ab=ab: eng.tensor_tensor(
                out_av[:, h * MYQ + k * 512:h * MYQ + (k + 1) * 512],
                av_ps[ab][:], stage_b[:, ab * 512:(ab + 1) * 512], ALU.mult),
                inc=ev)
            av_guard[ab] = (v_an, s_v)

    dma_drain()
    barrier()
    if DEBUG:
        dma("sync", ld, dbg_r2, row2[:])
        dma("sync", ld, dbg_lnl, row1[:])
        dma("sync", ld, dbg_stb, stage_b[:, 0:512])
        dma("sync", ld, dbg_gray, gray_bf[:])
        dma("sync", ld, dbg_av, out_av[:])
        dma("sync", ld, dbg_kf, kf[:])
        dma("sync", ld, dbg_v, v_sb[:])
        dma("sync", ld, dbg_qf, qf[:])
        dma("sync", ld, dbg_bias, bias_sb[:])
        dma_drain()
        barrier()
    pap.close()
    pa.close()
    pl.close()

    pwx = ExitStack()
    xnh = sbuf(pwx, "xnh", [128, DC * 1024], BF16)
    pw = ExitStack()
    ow_bf = sbuf(pw, "ow_bf", [128, DC * D], BF16)
    attn_sb = [sbuf(pw, f"attn_sb{i}", [128, D], F32) for i in range(2)]
    pwp = ExitStack()
    ow_ps = [psum(pwp, f"ow_ps{i}", [128, 512], F32) for i in range(4)]

    x_ld = [None] * DC
    x_ld[0] = dma("gpsimd", gp, attn_sb[0][:], xq_in[0:128, :])
    x_ld[1] = dma("gpsimd", gp, attn_sb[1][:], xq_in[128:256, :])
    stg_cast = {0: None, 1: None}
    stg_vc = [None] * DC
    for vc in range(DC):
        sti = vc % 2
        st = (stage_a, stage_b)[sti]
        wait("gpsimd", s_s, stg_cast[sti])
        vw = dma("gpsimd", gp, st[:], ow_w[vc])
        wait("scalar", s_gp, vw)
        stg_cast[sti] = stg_vc[vc] = op(
            "scalar", lambda eng, vc=vc, st=st: eng.copy(
                ow_bf[:, vc * D:(vc + 1) * D], st[:]), inc=es)
    xnh0_last = None
    for dc in range(DC):
        xnh0_last = dma("gpsimd", gp, xnh[:, dc * 1024:(dc + 1) * 1024],
                        xn_spill[dc, :, 0:1024])

    copy_evt = [None]
    dma_evt = [None, None]
    for qt in range(DC):
        wait("tensor", s_v, copy_evt[0])
        t_mm = None
        for vc in range(DC):
            if qt == 0:
                wait("tensor", s_s, stg_vc[vc])
            for dm in range(4):
                t_mm = op("tensor", lambda eng, qt=qt, vc=vc, dm=dm:
                          eng.matmul(
                              ow_ps[dm][:],
                              out_av[:, vc * MYQ + qt * 128:
                                     vc * MYQ + (qt + 1) * 128],
                              ow_bf[:, vc * D + dm * 512:
                                    vc * D + (dm + 1) * 512],
                              start=(vc == 0), stop=(vc == DC - 1)),
                          inc=(et if (vc == DC - 1 and dm == 3) else None))
        wait("vector", s_t, t_mm)
        ab = qt % 2
        wait("vector", s_gp, x_ld[qt])
        cp = None
        for dm in range(4):
            cp = op("vector", lambda eng, ab=ab, dm=dm: eng.tensor_tensor(
                attn_sb[ab][:, dm * 512:(dm + 1) * 512], ow_ps[dm][:],
                attn_sb[ab][:, dm * 512:(dm + 1) * 512], ALU.add),
                inc=(ev if dm == 3 else None))
        copy_evt[0] = cp
        wait("sync", s_v, cp)
        dma_evt[ab] = dma("sync", ld,
                          attn_scr[qt * 128:(qt + 1) * 128, :],
                          attn_sb[ab][:])
        if qt + 2 < DC:
            wait("gpsimd", s_ld, dma_evt[ab])
            x_ld[qt + 2] = dma("gpsimd", gp, attn_sb[ab][:],
                               xq_in[(qt + 2) * 128:(qt + 3) * 128, :])
    dma_drain()
    barrier()
    pwp.close()
    pw.close()
    pav.close()

    pf = ExitStack()
    ffn_sb = [sbuf(pf, f"ffn_sb{i}", [128, D], F32) for i in range(8)]
    h_all = sbuf(pf, "h_all", [128, 8 * 1024], BF16)
    w3_bf = sbuf(pf, "w3_bf", [128, 8 * D], BF16)
    w1_bf = [sbuf(pf, f"w1_bf{i}", [128, D], BF16) for i in range(2)]
    w2_bf = [sbuf(pf, f"w2_bf{i}", [128, D], BF16) for i in range(2)]
    w1st = [stage_a, stage_b]
    w2st = [sbuf(pf, f"w2st{i}", [128, D], F32) for i in range(2)]
    w3st = sbuf(pf, "w3st", [128, D], F32)
    silu_sb = [sbuf(pf, f"silu_sb{i}", [128, 512], BF16)
               for i in range(2)]
    pfp = ExitStack()
    g_ps = [psum(pfp, f"g_ps{i}", [128, 512], F32) for i in range(2)]
    v_ps_t = psum(pfp, "v_ps_t", [128, 512], F32)
    v_ps = [v_ps_t, v_ps_t]
    f_ps = [psum(pfp, f"f_ps{i}", [128, 512], F32) for i in range(4)]

    gate_done = [None]
    prev_cast = [None, None]
    out_dma = [None] * 8
    for half in range(2):
        if half == 1:
            wait("gpsimd", s_t, gate_done[0])
            xnh_last = None
            for dc in range(DC):
                xnh_last = dma("gpsimd", gp,
                               xnh[:, dc * 1024:(dc + 1) * 1024],
                               xn_spill[dc, :, 1024:2048])
        else:
            xnh_last = xnh0_last
        init_dma = [None] * 8
        for tg in range(8):
            rows = half * 1024 + tg * 128
            wait("sync", s_ld, out_dma[tg])
            init_dma[tg] = dma("sync", ld, ffn_sb[tg][:],
                               attn_scr[rows:rows + 128, :])
        cast_evt = [prev_cast[0], prev_cast[1]]
        wait("tensor", s_gp, xnh_last)

        silu_evt = [None, None]
        h_evt = [None, None]
        gatemm_evt = [None, None]
        w3cast_evt = [None]
        w3mm_evt = [None]
        stt_evt = [None]
        stt_final = [None] * 8
        for e in range(NE):
            for fc in range(8):
                bi = fc % 2
                wait("gpsimd", s_v, cast_evt[bi])
                v1 = dma("gpsimd", gp, w1st[bi][:], w1_w[e, fc])
                v2 = dma("gpsimd", gp, w2st[bi][:], w2_w[e, fc])
                wait("vector", s_gp, v2)
                wait("vector", s_t, gatemm_evt[bi])
                op("vector", lambda eng, bi=bi: eng.tensor_copy(
                    w1_bf[bi][:], w1st[bi][:]))
                v_c2 = op("vector", lambda eng, bi=bi: eng.tensor_copy(
                    w2_bf[bi][:], w2st[bi][:]), inc=ev)
                cast_evt[bi] = v_c2
                wait("tensor", s_v, v_c2)
                if e == 0 and fc == 0:
                    pass
                if fc == 0:
                    wait("vector", s_t, w3mm_evt[0])
                for ts in range(2):
                    gb = (fc * 2 + ts) % 2
                    wait("tensor", s_s, silu_evt[gb])
                    t_g = None
                    for dcc in range(DC):
                        t_g = op("tensor",
                                 lambda eng, bi=bi, dcc=dcc, ts=ts, gb=gb:
                                 eng.matmul(
                                     g_ps[gb][:], w1_bf[bi][:, dcc * 128:
                                                            (dcc + 1) * 128],
                                     xnh[:, dcc * 1024 + ts * 512:
                                         dcc * 1024 + (ts + 1) * 512],
                                     start=(dcc == 0), stop=(dcc == DC - 1)),
                                 inc=(et if dcc == DC - 1 else None))
                    wait("tensor", s_v, h_evt[0])
                    wait("tensor", s_v, h_evt[1])
                    t_v = None
                    for dcc in range(DC):
                        t_v = op("tensor",
                                 lambda eng, bi=bi, dcc=dcc, ts=ts, gb=gb:
                                 eng.matmul(
                                     v_ps[gb][:], w2_bf[bi][:, dcc * 128:
                                                            (dcc + 1) * 128],
                                     xnh[:, dcc * 1024 + ts * 512:
                                         dcc * 1024 + (ts + 1) * 512],
                                     start=(dcc == 0), stop=(dcc == DC - 1)),
                                 inc=(et if dcc == DC - 1 else None))
                    gatemm_evt[bi] = t_v
                    wait("scalar", s_t, t_g)
                    wait("scalar", s_v, h_evt[gb])
                    silu_evt[gb] = op("scalar", lambda eng, gb=gb:
                                      eng.activation(
                                          silu_sb[gb][:], g_ps[gb][:],
                                          AF.Silu), inc=es)
                    wait("vector", s_s, silu_evt[gb])
                    wait("vector", s_t, t_v)
                    h_evt[gb] = op(
                        "vector",
                        lambda eng, fc=fc, ts=ts, gb=gb: eng.tensor_tensor(
                            h_all[:, fc * 1024 + ts * 512:
                                  fc * 1024 + (ts + 1) * 512],
                            silu_sb[gb][:], v_ps[gb][:], ALU.mult), inc=ev)
                wait("gpsimd", s_s, w3cast_evt[0])
                v3 = dma("gpsimd", gp, w3st[:], w3_w[e, fc])
                wait("scalar", s_gp, v3)
                if fc == 0:
                    wait("scalar", s_t, w3mm_evt[0])
                w3cast_evt[0] = op(
                    "scalar", lambda eng, fc=fc: eng.copy(
                        w3_bf[:, fc * D:(fc + 1) * D], w3st[:]), inc=es)
            gate_done[0] = gatemm_evt[(8 - 1) % 2]
            wait("tensor", s_s, w3cast_evt[0])
            wait("tensor", s_v, h_evt[0])
            wait("tensor", s_v, h_evt[1])
            for tg in range(8):
                wait("tensor", s_v, stt_evt[0])
                t_f = None
                for fc in range(8):
                    for dm in range(4):
                        t_f = op("tensor",
                                 lambda eng, tg=tg, fc=fc, dm=dm:
                                 eng.matmul(
                                     f_ps[dm][:],
                                     h_all[:, fc * 1024 + tg * 128:
                                           fc * 1024 + (tg + 1) * 128],
                                     w3_bf[:, fc * D + dm * 512:
                                           fc * D + (dm + 1) * 512],
                                     start=(fc == 0), stop=(fc == 7)),
                                 inc=(et if (fc == 7 and dm == 3) else None))
                wait("vector", s_t, t_f)
                if e == 0:
                    wait("vector", s_ld, init_dma[tg])
                vv = None
                for dm in range(4):
                    col = (half * 8 + tg) * 8 + e
                    vv = op("vector",
                            lambda eng, tg=tg, dm=dm, col=col:
                            eng.scalar_tensor_tensor(
                                ffn_sb[tg][:, dm * 512:(dm + 1) * 512],
                                f_ps[dm][:], g_all[:, col:col + 1],
                                ffn_sb[tg][:, dm * 512:(dm + 1) * 512],
                                ALU.mult, ALU.add),
                            inc=(ev if dm == 3 else None))
                stt_evt[0] = vv
                stt_final[tg] = vv
            w3mm_evt[0] = et.n
        prev_cast[0], prev_cast[1] = cast_evt[0], cast_evt[1]
        for tg in range(8):
            wait("sync", s_v, stt_final[tg])
            out_dma[tg] = dma("sync", ld,
                              out_hbm[half * 1024 + tg * 128:
                                      half * 1024 + (tg + 1) * 128, :],
                              ffn_sb[tg][:])
    if DEBUG:
        for dc in range(DC):
            dma("sync", ld, dbg_xnsp[dc], xn_spill[dc])
        for i in range(4):
            dma("sync", ld, dbg_attn[i * 512:(i + 1) * 512, :],
                attn_scr[i * 512:(i + 1) * 512, :])
        dma_drain()
        barrier()
    dma_drain()
    barrier()
    pfp.close()
    pf.close()
    pwx.close()
    root.close()

    with nc.Block() as blk:
        @blk.sync
        def _(eng):
            for fn in sched["sync"]:
                fn(eng)

        @blk.gpsimd
        def _(eng):
            for fn in sched["gpsimd"]:
                fn(eng)

        @blk.vector
        def _(eng):
            for fn in sched["vector"]:
                fn(eng)

        @blk.scalar
        def _(eng):
            for fn in sched["scalar"]:
                fn(eng)

        @blk.tensor
        def _(eng):
            for fn in sched["tensor"]:
                fn(eng)
    return nc


def _host_tables():
    i = np.arange(32, dtype=np.float64)
    inv = THETA ** (-2.0 * i / 64.0)
    t = np.arange(S, dtype=np.float64)
    ang = np.outer(inv, t)
    cosp = np.repeat(np.cos(ang), 2, axis=0)
    sinp = np.repeat(np.sin(ang), 2, axis=0)
    sinp[0::2] *= -1.0
    cos128 = np.tile(cosp, (2, 1)).astype(np.float32)
    sin128 = np.tile(sinp, (2, 1)).astype(np.float32)
    diag = np.where(np.arange(128)[:, None] <= np.arange(128)[None, :],
                    np.float32(0.0), np.float32(NEG))
    return cos128, sin128, diag


def _gray_kill(par, diag):
    gray = np.zeros((8, 128, 512), np.float32)
    for g in range(8):
        if par == 0 and g < 4:
            geff = g
        elif par == 1 and g >= 4:
            geff = g - 4
        else:
            continue
        gray[g, :, :128 * geff] = NEG
        gray[g, :, 128 * geff:128 * (geff + 1)] = diag
    kill = np.zeros((NT, 128, 32), np.float32)
    if par == 0:
        for k in range(NT):
            kill[k, :, 8 * k + 4:8 * k + 8] = 1.0
    return gray, kill


def make_in_maps(L0_attn_x, L0_attn_q_a_w, L0_attn_q_a_ln_w, L0_attn_q_b_w,
                 L0_attn_kv_a_w, L0_attn_kv_a_ln_w, L0_attn_kv_b_w,
                 L0_attn_o_w, L0_ffn_router_w, L0_ffn_w1, L0_ffn_w2,
                 L0_ffn_w3):
    f32 = lambda a: np.ascontiguousarray(np.asarray(a), dtype=np.float32)
    x = f32(L0_attn_x)
    qa = f32(L0_attn_q_a_w)
    qaln = f32(L0_attn_q_a_ln_w)
    qb = f32(L0_attn_q_b_w)
    kva = f32(L0_attn_kv_a_w)
    kvln = f32(L0_attn_kv_a_ln_w)
    kvb = f32(L0_attn_kv_b_w)
    ow = f32(L0_attn_o_w)
    rw = f32(L0_ffn_router_w)
    w1 = f32(L0_ffn_w1)
    w2 = f32(L0_ffn_w2)
    w3 = f32(L0_ffn_w3)

    qa_p = np.zeros((D, QLORA_P), np.float32)
    qa_p[:, :QLORA] = qa
    qaln_p = np.zeros((QLORA_P,), np.float32)
    qaln_p[:QLORA] = qaln
    qaln_p = np.ascontiguousarray(qaln_p.reshape(LC, 128).T)
    qb_p = np.zeros((QLORA_P, D), np.float32)
    qb_p[:QLORA, :] = qb

    qa_h = np.ascontiguousarray(qa_p.reshape(DC, 128, QLORA_P))
    qb_h = np.ascontiguousarray(
        qb_p.reshape(LC, 128, NH, 128).transpose(2, 1, 0, 3).reshape(
            NH, 128, LC * 128))
    kva_h = np.ascontiguousarray(kva.reshape(DC, 128, 320))
    kvln_h = np.ascontiguousarray(kvln.reshape(2, 128).T)
    kvb_h = np.ascontiguousarray(kvb.reshape(2, 128, NH * 192))
    ow_h = np.ascontiguousarray(ow.reshape(DC, 128, D))
    rw_h = np.ascontiguousarray(rw.T.reshape(DC, 128, NE))
    w1_h = np.ascontiguousarray(
        w1.reshape(NE, DC, 128, 8, 128).transpose(0, 3, 2, 1, 4).reshape(
            NE, 8, 128, DC * 128))
    w2_h = np.ascontiguousarray(
        w2.reshape(NE, DC, 128, 8, 128).transpose(0, 3, 2, 1, 4).reshape(
            NE, 8, 128, DC * 128))
    w3_h = np.ascontiguousarray(w3.reshape(NE, 8, 128, D))

    cos128, sin128, diag = _host_tables()
    ident = np.eye(128, dtype=np.float32)

    offs = {0: [0, 1024, 2048, 3072], 1: [512, 1536, 2560, 3584]}
    in_maps = []
    for core in range(8):
        b, par = core // 2, core % 2
        myrows = np.concatenate(
            [np.arange(o, o + 512) for o in offs[par]])
        gray, kill = _gray_kill(par, diag)
        in_maps.append({
            "x": np.ascontiguousarray(x[b]),
            "xq": np.ascontiguousarray(x[b][myrows]),
            "qa_w": qa_h, "qa_ln": qaln_p, "qb_w": qb_h,
            "kva_w": kva_h, "kva_ln": kvln_h, "kvb_w": kvb_h,
            "ow_w": ow_h, "rw_t": rw_h,
            "w1": w1_h, "w2": w2_h, "w3": w3_h,
            "cos_k": cos128, "sin_k": sin128,
            "cos_q": np.ascontiguousarray(cos128[:, myrows]),
            "sin_q": np.ascontiguousarray(sin128[:, myrows]),
            "graymask": gray.astype(ml_dtypes.bfloat16), "killf": kill,
            "ident_in": ident,
        })
    return in_maps


def kernel(**inputs):
    in_maps = make_in_maps(**inputs)
    nc = build_program()
    res = run_bass_kernel_spmd(nc, in_maps, core_ids=list(range(8)))
    offs = {0: [0, 1024, 2048, 3072], 1: [512, 1536, 2560, 3584]}
    out = np.empty((B, S, D), np.float32)
    for core in range(8):
        b, par = core // 2, core % 2
        o = np.asarray(res.results[core]["out"])
        for k, off in enumerate(offs[par]):
            out[b, off:off + 512] = o[k * 512:(k + 1) * 512]
    return out
